# version 01 (final)
# baseline (speedup 1.0000x reference)
import contextlib
import numpy as np
import contextlib
import numpy as np
import bass_rust
import concourse.bass as bass
import concourse.mybir as mybir
from concourse.bass_utils import run_bass_kernel_spmd

F32 = mybir.dt.float32
BF16 = mybir.dt.bfloat16
AF = mybir.ActivationFunctionType
ALU = mybir.AluOpType
AX = mybir.AxisListType

ENGINES = ("sync", "scalar", "vector", "tensor", "gpsimd")


class Sem:
    def __init__(self, handle, name):
        self.h = handle
        self.name = name
        self.v = 0


class Prog:
    paranoid = False

    def __init__(self, nc, es):
        self.nc = nc
        self.es = es
        self.items = {e: [] for e in ENGINES}
        self.sems = {}

    def sem(self, name):
        s = Sem(self.es.enter_context(self.nc.semaphore(name)), name)
        self.sems[name] = s
        return s

    def op(self, engine, fn, waits=(), inc=None, inc_k=1):
        if Prog.paranoid:
            ws = [(s, s.v) for s in self.sems.values()]
        else:
            ws = []
            for w in waits:
                if isinstance(w, Sem):
                    ws.append((w, w.v))
                else:
                    ws.append((w[0], w[1]))
        self.items[engine].append((ws, fn, inc, inc_k))
        if inc is not None:
            inc.v += inc_k

    def dma(self, queue, out, in_, sem, waits=(), slow=False):
        def fn(eng):
            if slow:
                return eng.dma_start(out=out, in_=in_,
                                     allow_slow_non_contiguous=True)
            return eng.dma_start(out=out, in_=in_)
        self.op(queue, fn, waits=waits, inc=sem, inc_k=16)

    def barrier_waits(self):
        return [(s, s.v) for s in self.sems.values()]

    def build_block(self, block):
        for engine in ENGINES:
            items = self.items[engine]
            if not items:
                continue

            def make(engine_items):
                def emit(eng):
                    for ws, fn, inc, inc_k in engine_items:
                        for s, v in ws:
                            if v > 0:
                                eng.wait_ge(s.h, v)
                        instr = fn(eng)
                        if inc is not None:
                            assert instr is not None
                            instr.then_inc(inc.h, inc_k)
                return emit

            getattr(block, engine)(make(items))


def ap(handle_ap, offset, dims):
    return bass_rust.AP(handle_ap.tensor, offset, dims)


def vap(view, extra_offset, free_dims, nparts=None):
    v = view if hasattr(view, "ap") and not callable(getattr(view, "ap")) else view
    a = view.ap
    pstride = a[0][0]
    np_ = nparts if nparts is not None else a[0][1]
    return bass_rust.AP(view.tensor, view.offset + extra_offset,
                        [[pstride, np_]] + free_dims)


def dram_in(nc, name, shape):
    return nc.dram_tensor(name, list(shape), F32, kind="ExternalInput").ap()


def dram_out(nc, name, shape):
    return nc.dram_tensor(name, list(shape), F32, kind="ExternalOutput").ap()

H = 4096
L = 2048
SW = 512
NS = 4
HT = 32
ZT = 64
DT = 32
CH = 256
NCH = 8
N = 16
T1 = 1024
EPS_LN = 1e-5



def build_p1():
    nc = bass.Bass("TRN2", target_bir_lowering=False)
    xT = dram_in(nc, "xT", [H, L])
    winTu = dram_in(nc, "winTu", [H, H])
    winTg = dram_in(nc, "winTg", [H, H])
    convw = dram_in(nc, "convw", [128, 128])
    convb = dram_in(nc, "convb", [128, 32])
    wxT3 = dram_in(nc, "wxT3", [HT, 128, 288])
    onescol = dram_in(nc, "onescol", [128, 1])
    onesrow = dram_in(nc, "onesrow", [1, 128])
    cvec = dram_in(nc, "cvec", [128, 4])

    o_xn = dram_out(nc, "xn", [H, L])
    o_u = dram_out(nc, "u", [H, L])
    o_sg = dram_out(nc, "sg", [H, L])
    o_sp = [dram_out(nc, "sp0", [128, L]), dram_out(nc, "sp1", [128, L]),
            dram_out(nc, "sp2", [32, L])]

    with contextlib.ExitStack() as es:
        sb = lambda name, shp, dt=F32: es.enter_context(nc.sbuf_tensor(name, shp, dt))
        ps = lambda name, shp: es.enter_context(nc.psum_tensor(name, shp, F32))

        xslab = sb("xslab", [128, HT * SW])
        xnslab = sb("xnslab", [128, HT * SW])
        xsq = [sb(f"xsq{i}", [128, SW]) for i in range(2)]
        rowb = sb("rowb", [1, 8 * SW])
        ocol = sb("ocol", [128, 1])
        orow = sb("orow", [1, 128])
        cv = sb("cv", [128, 4])
        cwsb = sb("cwsb", [128, 128])
        cbsb = sb("cbsb", [128, 32])
        wbuf = [sb(f"wbuf{i}", [128, HT * 128]) for i in range(2)]
        wxt = [sb(f"wxt{i}", [128, 288]) for i in range(2)]
        u0t = sb("u0t", [128, 515])
        acc = [sb(f"acc{i}", [128, SW]) for i in range(2)]
        ut = [sb(f"ut{i}", [128, SW]) for i in range(2)]
        sgt = [sb(f"sgt{i}", [128, SW]) for i in range(2)]
        halo = sb("halo", [128, 32 * 3])
        spsb = [sb("sp0sb", [128, SW]), sb("sp1sb", [128, SW]), sb("sp2sb", [32, SW])]
        tmpv = sb("tmpv", [128, SW])

        pp = [ps(f"pp{i}", [128, SW]) for i in range(6)]

        P = Prog(nc, es)
        s_ld = P.sem("s_ld")
        s_st = P.sem("s_st")
        s_t = P.sem("s_t")
        s_v = P.sem("s_v")
        s_s = P.sem("s_s")

        P.dma("sync", ocol[:, :], onescol, s_ld)
        P.dma("sync", orow[:, :], onesrow, s_ld)
        P.dma("sync", cv[:, :], cvec, s_ld)
        P.dma("sync", cwsb[:, :], convw, s_ld)
        P.dma("sync", cbsb[:, :], convb, s_ld)
        P.op("vector", lambda eng: eng.memset(halo[:, :], 0.0), inc=s_v)
        v_memset = s_v.v

        v_after_xn = {}
        t_wbuf_free = {}
        s_pmm_free = {}
        v_u0t_free = 0
        st_ut_free = {}
        t_wx_done = {}

        for s in range(NS):
            ld_tile = {}
            for h in range(HT):
                waits = []
                if s > 0:
                    waits.append((s_v, v_after_xn[s - 1]))
                P.dma("sync", xslab[:, h * SW:(h + 1) * SW],
                      ap(xT, (h * 128) * L + s * SW, [[L, 128], [1, SW]]),
                      s_ld, waits=waits)
                ld_tile[h] = s_ld.v

            t_sq_cons = {}
            for h in range(HT):
                xs = xslab[:, h * SW:(h + 1) * SW]
                sq = xsq[h % 2]
                waits = [(s_ld, ld_tile[h])]
                if h >= 2:
                    waits.append((s_t, t_sq_cons[h - 2]))
                P.op("scalar",
                     (lambda sq=sq, xs=xs: (lambda eng: eng.activation(sq[:, :], xs, AF.Square)))(),
                     waits=waits, inc=s_s)
                s_sq = s_s.v

                def statmm(eng, xs=xs, sq=sq, st=(h == 0), sp=(h == HT - 1)):
                    eng.matmul(pp[0][0:1, :], ocol[:, :], xs, start=st, stop=sp)
                    return eng.matmul(pp[1][0:1, :], ocol[:, :], sq[:, :], start=st, stop=sp)
                P.op("tensor", statmm, waits=[(s_s, s_sq)], inc=s_t)
                t_sq_cons[h] = s_t.v
            t_stats = s_t.v

            r_m = rowb[:, 0:SW]
            r_a = rowb[:, SW:2 * SW]
            r_b = rowb[:, 2 * SW:3 * SW]
            r_var = rowb[:, 3 * SW:4 * SW]
            r_ln = rowb[:, 4 * SW:5 * SW]
            r_rstd = rowb[:, 5 * SW:6 * SW]

            def stats_v(eng):
                eng.tensor_scalar_mul(r_m, pp[0][0:1, :], 1.0 / H)
                eng.tensor_scalar_mul(r_a, pp[1][0:1, :], 1.0 / H)
                eng.tensor_tensor(r_b, r_m, r_m, ALU.mult)
                return eng.tensor_tensor(r_var, r_a, r_b, ALU.subtract)
            P.op("vector", stats_v, waits=[(s_t, t_stats)], inc=s_v)
            v_stats = s_v.v

            def stats_s(eng):
                eng.activation(r_ln, r_var, AF.Ln, bias=cv[0:1, 0:1])
                return eng.activation(r_rstd, r_ln, AF.Exp, scale=-0.5)
            P.op("scalar", stats_s, waits=[(s_v, v_stats)], inc=s_s)
            s_rstd = s_s.v

            def repmm(eng):
                eng.matmul(pp[2][:, :], orow[:, :], r_m, start=True, stop=True)
                return eng.matmul(pp[3][:, :], orow[:, :], r_rstd, start=True, stop=True)
            P.op("tensor", repmm, waits=[(s_s, s_rstd)], inc=s_t)
            t_rep = s_t.v

            for h in range(HT):
                xs = xslab[:, h * SW:(h + 1) * SW]
                xn_h = xnslab[:, h * SW:(h + 1) * SW]
                waits = [(s_t, t_rep)]
                if s > 0:
                    waits.append((s_t, t_rep))
                def xn_v(eng, xs=xs, xn_h=xn_h):
                    eng.tensor_tensor(tmpv[:, :], xs, pp[2][:, :], ALU.subtract)
                    return eng.tensor_tensor(xn_h, tmpv[:, :], pp[3][:, :], ALU.mult)
                P.op("vector", xn_v, waits=waits, inc=s_v)
                P.dma("gpsimd", ap(o_xn, (h * 128) * L + s * SW, [[L, 128], [1, SW]]),
                      xn_h, s_st, waits=[(s_v, s_v.v)])
            v_after_xn[s] = s_v.v

            ld_wbuf = {}
            ld_wxt = {}
            for oi in range(64):
                src = winTu if oi < 32 else winTg
                co = (oi % 32) * 128
                waits = []
                key = (s, oi - 2) if oi >= 2 else ((s - 1, 62 + oi) if s > 0 else None)
                if key is not None:
                    waits.append((s_t, t_wbuf_free[key]))
                P.dma("sync", wbuf[oi % 2][:, :],
                      ap(src, co, [[H, 128], [128 * H, HT], [1, 128]]),
                      s_ld, waits=waits)
                ld_wbuf[oi] = s_ld.v
                if oi < 32:
                    waits = []
                    keyx = (s, oi - 2) if oi >= 2 else ((s - 1, 30 + oi) if s > 0 else None)
                    if keyx is not None:
                        waits.append((s_t, t_wx_done[keyx]))
                    P.dma("sync", wxt[oi % 2][:, :],
                          ap(wxT3, oi * 128 * 288, [[288, 128], [1, 288]]),
                          s_ld, waits=waits)
                    ld_wxt[oi] = s_ld.v
                bank = pp[4 + (oi % 2)]
                waits = [(s_ld, ld_wbuf[oi]), (s_v, v_after_xn[s])]
                if (s, oi) in s_pmm_free or (s - 1, oi) in s_pmm_free:
                    pass
                key = (s, oi - 2) if oi >= 2 else ((s - 1, 62 + oi) if s > 0 else None)
                if key is not None:
                    waits.append((s_s, s_pmm_free[key]))

                def winmm(eng, oi=oi, bank=bank):
                    last = None
                    for h in range(HT):
                        last = eng.matmul(bank[:, :],
                                          wbuf[oi % 2][:, h * 128:(h + 1) * 128],
                                          xnslab[:, h * SW:(h + 1) * SW],
                                          start=(h == 0), stop=(h == HT - 1))
                    return last
                P.op("tensor", winmm, waits=waits, inc=s_t)
                t_wbuf_free[(s, oi)] = s_t.v
                t_win = s_t.v

                if oi < 32:
                    waits = [(s_t, t_win), (s_v, v_u0t_free)]
                    P.op("scalar",
                         (lambda bank=bank: (lambda eng: eng.copy(u0t[:, 3:515], bank[:, :])))(),
                         waits=waits, inc=s_s)
                    s_pmm_free[(s, oi)] = s_s.v
                    s_evac = s_s.v

                    def convv(eng, oi=oi, s=s):
                        eng.tensor_copy(u0t[:, 0:3], halo[:, oi * 3:oi * 3 + 3])
                        bias_b = cbsb[:, oi:oi + 1].broadcast_to([128, SW])
                        eng.scalar_tensor_tensor(acc[0][:, :], u0t[:, 0:SW],
                                                 cwsb[:, oi * 4 + 0:oi * 4 + 1],
                                                 bias_b, ALU.mult, ALU.add)
                        eng.scalar_tensor_tensor(acc[1][:, :], u0t[:, 1:1 + SW],
                                                 cwsb[:, oi * 4 + 1:oi * 4 + 2],
                                                 acc[0][:, :], ALU.mult, ALU.add)
                        eng.scalar_tensor_tensor(acc[0][:, :], u0t[:, 2:2 + SW],
                                                 cwsb[:, oi * 4 + 2:oi * 4 + 3],
                                                 acc[1][:, :], ALU.mult, ALU.add)
                        eng.scalar_tensor_tensor(acc[1][:, :], u0t[:, 3:3 + SW],
                                                 cwsb[:, oi * 4 + 3:oi * 4 + 4],
                                                 acc[0][:, :], ALU.mult, ALU.add)
                        return eng.tensor_copy(halo[:, oi * 3:oi * 3 + 3], u0t[:, 512:515])
                    P.op("vector", convv, waits=[(s_s, s_evac)], inc=s_v)
                    v_u0t_free = s_v.v
                    v_conv = s_v.v

                    waits = [(s_v, v_conv)]
                    key = (s, oi - 2) if oi >= 2 else ((s - 1, oi + 30) if s > 0 else None)
                    if key is not None:
                        waits.append((s_st, st_ut_free[key]))
                        waits.append((s_t, t_wx_done[key]))
                    P.op("scalar",
                         (lambda oi=oi: (lambda eng: eng.activation(ut[oi % 2][:, :], acc[1][:, :], AF.Silu)))(),
                         waits=waits, inc=s_s)
                    s_silu = s_s.v

                    def wxmm(eng, oi=oi, s=s):
                        st0 = (oi == 0)
                        sp0_ = (oi == 31)
                        eng.matmul(pp[0][:, :], wxt[oi % 2][:, 0:128], ut[oi % 2][:, :],
                                   start=st0, stop=sp0_)
                        eng.matmul(pp[1][:, :], wxt[oi % 2][:, 128:256], ut[oi % 2][:, :],
                                   start=st0, stop=sp0_)
                        return eng.matmul(pp[2][0:32, :], wxt[oi % 2][:, 256:288], ut[oi % 2][:, :],
                                          start=st0, stop=sp0_)
                    P.op("tensor", wxmm, waits=[(s_s, s_silu), (s_ld, ld_wxt[oi])], inc=s_t)
                    t_wx_done[(s, oi)] = s_t.v

                    P.dma("gpsimd", ap(o_u, (oi * 128) * L + s * SW, [[L, 128], [1, SW]]),
                          ut[oi % 2][:, :], s_st, waits=[(s_s, s_silu)])
                    st_ut_free[(s, oi)] = s_st.v
                else:
                    gi = oi - 32
                    waits = [(s_t, t_win)]
                    key = (s, oi - 2) if oi >= 34 else ((s - 1, oi + 30) if s > 0 else None)
                    if key is not None:
                        waits.append((s_st, st_ut_free[key]))
                    P.op("scalar",
                         (lambda oi=oi, bank=bank: (lambda eng: eng.activation(sgt[oi % 2][:, :], bank[:, :], AF.Silu)))(),
                         waits=waits, inc=s_s)
                    s_pmm_free[(s, oi)] = s_s.v
                    s_silu = s_s.v
                    P.dma("gpsimd", ap(o_sg, (gi * 128) * L + s * SW, [[L, 128], [1, SW]]),
                          sgt[oi % 2][:, :], s_st, waits=[(s_s, s_silu)])
                    st_ut_free[(s, oi)] = s_st.v

            t_last_wx = s_t.v
            waits = [(s_t, t_last_wx)]
            if s > 0:
                waits.append((s_st, st_sp_done))

            def spevac(eng, s=s):
                eng.copy(spsb[0][:, :], pp[0][:, :])
                eng.copy(spsb[1][:, :], pp[1][:, :])
                return eng.copy(spsb[2][:, :], pp[2][0:32, :])
            P.op("scalar", spevac, waits=waits, inc=s_s)
            s_spevac = s_s.v
            rows = [128, 128, 32]
            for i in range(3):
                P.dma("gpsimd", ap(o_sp[i], s * SW, [[L, rows[i]], [1, SW]]),
                      spsb[i][:, :], s_st, waits=[(s_s, s_spevac)])
            st_sp_done = s_st.v
            if s < NS - 1:
                P.op("tensor", lambda eng: None, waits=[(s_s, s_spevac)])

        P.op("gpsimd", lambda eng: None, waits=[(s_st, s_st.v)])

        block = es.enter_context(nc.Block())
        P.build_block(block)

    return nc


def prep_p1(inputs, cid):
    b, dh = cid // 2, cid % 2
    d0 = dh * H
    x = np.asarray(inputs["L0_attn_hidden_states"], np.float32)
    W = np.asarray(inputs["L0_attn_W_in"], np.float32)
    cw = np.asarray(inputs["L0_attn_conv_w"], np.float32)
    cb = np.asarray(inputs["L0_attn_conv_b"], np.float32)
    Wx = np.asarray(inputs["L0_attn_Wx"], np.float32)
    m = {
        "xT": np.ascontiguousarray(x[b].T),
        "winTu": np.ascontiguousarray(W[d0:d0 + H, :].T),
        "winTg": np.ascontiguousarray(W[8192 + d0:8192 + d0 + H, :].T),
        "convw": np.ascontiguousarray(
            cw[d0:d0 + H].reshape(32, 128, 4).transpose(1, 0, 2).reshape(128, 128)),
        "convb": np.ascontiguousarray(cb[d0:d0 + H].reshape(32, 128).T),
        "wxT3": np.ascontiguousarray(
            Wx[:, d0:d0 + H].T.reshape(32, 128, 288)),
        "onescol": np.ones((128, 1), np.float32),
        "onesrow": np.ones((1, 128), np.float32),
        "cvec": np.tile(np.array([[1e-5, 1e-6, 1.0, 0.0]], np.float32), (128, 1)),
    }
    return m



def build_p2(debug=False):
    nc = bass.Bass("TRN2", target_bir_lowering=False)
    i_u = dram_in(nc, "u", [H, L])
    i_sg = dram_in(nc, "sg", [H, L])
    i_spa = [dram_in(nc, "spa0", [128, L]), dram_in(nc, "spa1", [128, L]),
             dram_in(nc, "spa2", [32, L])]
    i_spb = [dram_in(nc, "spb0", [128, L]), dram_in(nc, "spb1", [128, L]),
             dram_in(nc, "spb2", [32, L])]
    i_dtlnw = dram_in(nc, "dtlnw", [128, 2])
    i_blnw = dram_in(nc, "blnw", [16, 1])
    i_clnw = dram_in(nc, "clnw", [16, 1])
    i_wdtT = dram_in(nc, "wdtT", [2, 128, H])
    i_dtbias = dram_in(nc, "dtbias", [128, DT])
    i_A = dram_in(nc, "Acol", [128, DT * N])
    i_dparam = dram_in(nc, "dparam", [128, DT])
    i_ones = dram_in(nc, "onescol", [128, 1])
    i_orow = dram_in(nc, "onesrow", [1, 128])
    i_cv = dram_in(nc, "cvec", [128, 4])

    o_z = dram_out(nc, "z", [H, L])

    kind = dict(kind="ExternalOutput") if debug else {}
    dH = nc.dram_tensor("dH", [H, L], F32, **kind).ap()
    duH = nc.dram_tensor("duH", [H, L], F32, **kind).ap()
    bnH = nc.dram_tensor("bnH", [16, L], F32, **kind).ap()
    cnH = nc.dram_tensor("cnH", [16, L], F32, **kind).ap()

    with contextlib.ExitStack() as es:
        sb = lambda name, shp, dt=F32: es.enter_context(nc.sbuf_tensor(name, shp, dt))
        ps = lambda name, shp: es.enter_context(nc.psum_tensor(name, shp, F32))

        ssm0 = sb("ssm0", [128, L])
        ssm1 = sb("ssm1", [128, L])
        ssm2 = sb("ssm2", [32, L])
        cbuf = sb("cbuf", [16, L])
        pbt = sb("pbt", [128, 2 * L])
        pct = sb("pct", [16, L])
        sqt = sb("sqt", [128, 2 * SW])
        sqbc = sb("sqbc", [16, 2 * SW])
        rows = sb("rows", [1, 4 * SW])
        dtn0 = sb("dtn0", [128, L])
        dtn1 = sb("dtn1", [128, L])
        bcn = sb("bcn", [16, 2 * L])
        wdtb = sb("wdtb", [128, 2 * 128])
        dtbias = sb("dtbias_sb", [128, DT])
        acol = sb("acol", [128, DT * N])
        dpar = sb("dpar", [128, DT])
        ocol = sb("ocol", [128, 1])
        orow = sb("orow", [1, 128])
        cv = sb("cv", [128, 4])
        dtw = sb("dtw", [128, 2])
        blnw = sb("blnw_sb", [16, 1])
        clnw = sb("clnw_sb", [16, 1])
        dsl = sb("dsl", [128, SW])
        dusl = sb("dusl", [128, SW])
        usl = sb("usl", [128, SW])
        t_a = sb("t_a", [128, SW])
        t_b = sb("t_b", [128, SW])
        dab = sb("dab", [128, N * CH])
        wb = sb("wb", [128, N * CH])
        stb = sb("stb", [128, N * CH])
        brep = sb("brep", [128, N * CH])
        crep = sb("crep", [128, N * CH])
        carry = sb("carry", [128, DT * N])
        dchunk = sb("dchunk", [128, CH])
        duchunk = sb("duchunk", [128, CH])
        uchunk = sb("uchunk", [128, CH])
        sgchunk = sb("sgchunk", [128, CH])
        ychunk = sb("ychunk", [128, CH])
        zchunk = sb("zchunk", [128, CH])

        pp = [ps(f"pp{i}", [128, 512]) for i in range(4)]

        P = Prog(nc, es)
        s_ld = P.sem("s_ld")
        s_st = P.sem("s_st")
        s_t = P.sem("s_t")
        s_v = P.sem("s_v")
        s_s = P.sem("s_s")

        for dst, src in [(ocol, i_ones), (orow, i_orow), (cv, i_cv),
                         (dtw, i_dtlnw), (blnw, i_blnw), (clnw, i_clnw),
                         (dtbias, i_dtbias), (acol, i_A), (dpar, i_dparam)]:
            P.dma("sync", dst[:, :], src, s_ld)
        for dst, src in [(ssm0, i_spa[0]), (ssm1, i_spa[1]), (ssm2, i_spa[2])]:
            P.dma("sync", dst[:, :], src, s_ld)
        P.dma("sync", pbt[:, 0:L], i_spb[0], s_ld)
        P.dma("sync", pbt[:, L:2 * L], i_spb[1], s_ld)
        P.dma("sync", cbuf[:, :], ap(i_spb[2], 0, [[L, 16], [1, L]]), s_ld)
        P.dma("sync", pct[:, :], ap(i_spb[2], 16 * L, [[L, 16], [1, L]]), s_ld)
        ld_b = s_ld.v

        def comb(eng):
            eng.tensor_tensor(ssm0[:, :], ssm0[:, :], pbt[:, 0:L], ALU.add)
            eng.tensor_tensor(ssm1[:, :], ssm1[:, :], pbt[:, L:2 * L], ALU.add)
            return eng.tensor_tensor(ssm2[0:16, :], ssm2[0:16, :], cbuf[:, :], ALU.add)
        P.op("vector", comb, waits=[(s_ld, ld_b)], inc=s_v)
        v_comb = s_v.v
        P.dma("gpsimd", cbuf[:, :], ssm2[16:32, :], s_st, waits=[(s_v, v_comb)])
        st_ccopy = s_st.v
        P.op("vector",
             lambda eng: eng.tensor_tensor(cbuf[:, :], cbuf[:, :], pct[:, :], ALU.add),
             waits=[(s_st, st_ccopy)], inc=s_v)
        v_comb2 = s_v.v

        r0 = rows[:, 0:SW]
        r1 = rows[:, SW:2 * SW]
        r2 = rows[:, 2 * SW:3 * SW]
        r3 = rows[:, 3 * SW:4 * SW]

        for sl in range(4):
            c0 = sl * SW

            def sqs(eng, c0=c0):
                eng.activation(sqt[:, 0:SW], ssm0[:, c0:c0 + SW], AF.Square)
                eng.activation(sqt[:, SW:2 * SW], ssm1[:, c0:c0 + SW], AF.Square)
                eng.activation(sqbc[:, 0:SW], ssm2[0:16, c0:c0 + SW], AF.Square)
                return eng.activation(sqbc[:, SW:2 * SW], cbuf[:, c0:c0 + SW], AF.Square)
            P.op("scalar", sqs, waits=[(s_v, v_comb2), (s_t, s_t.v)], inc=s_s)
            s_sq = s_s.v

            def statmm(eng):
                eng.matmul(pp[0][0:1, :], ocol[:, :], sqt[:, 0:SW], start=True, stop=False)
                eng.matmul(pp[0][0:1, :], ocol[:, :], sqt[:, SW:2 * SW], start=False, stop=True)
                eng.matmul(pp[1][0:1, :], ocol[0:16, :], sqbc[:, 0:SW], start=True, stop=True)
                return eng.matmul(pp[2][0:1, :], ocol[0:16, :], sqbc[:, SW:2 * SW],
                                  start=True, stop=True)
            P.op("tensor", statmm, waits=[(s_s, s_sq), (s_v, s_v.v)], inc=s_t)
            t_mm = s_t.v

            def evac(eng):
                eng.tensor_scalar_mul(r0[0:1, :], pp[0][0:1, :], 1.0 / 256)
                eng.tensor_scalar_mul(r1[0:1, :], pp[1][0:1, :], 1.0 / 16)
                return eng.tensor_scalar_mul(r2[0:1, :], pp[2][0:1, :], 1.0 / 16)
            P.op("vector", evac, waits=[(s_t, t_mm)], inc=s_v)
            v_ev = s_v.v

            def rstds(eng):
                eng.activation(r3, r0, AF.Ln, bias=cv[0:1, 1:2])
                eng.activation(r0, r3, AF.Exp, scale=-0.5)
                eng.activation(r3, r1, AF.Ln, bias=cv[0:1, 1:2])
                eng.activation(r1, r3, AF.Exp, scale=-0.5)
                eng.activation(r3, r2, AF.Ln, bias=cv[0:1, 1:2])
                return eng.activation(r2, r3, AF.Exp, scale=-0.5)
            P.op("scalar", rstds, waits=[(s_v, v_ev)], inc=s_s)
            s_r = s_s.v

            def repmm(eng):
                eng.matmul(pp[0][:, :], orow[:, :], r0[0:1, :], start=True, stop=True)
                eng.matmul(pp[1][0:16, :], orow[:, 0:16], r1[0:1, :], start=True, stop=True)
                return eng.matmul(pp[2][0:16, :], orow[:, 0:16], r2[0:1, :],
                                  start=True, stop=True)
            P.op("tensor", repmm, waits=[(s_s, s_r)], inc=s_t)
            t_rep = s_t.v

            def normv(eng, c0=c0):
                eng.scalar_tensor_tensor(dtn0[:, c0:c0 + SW], ssm0[:, c0:c0 + SW],
                                         dtw[:, 0:1], pp[0][:, :], ALU.mult, ALU.mult)
                eng.scalar_tensor_tensor(dtn1[:, c0:c0 + SW], ssm1[:, c0:c0 + SW],
                                         dtw[:, 1:2], pp[0][:, :], ALU.mult, ALU.mult)
                eng.scalar_tensor_tensor(bcn[0:16, c0:c0 + SW], ssm2[0:16, c0:c0 + SW],
                                         blnw[:, 0:1], pp[1][0:16, :], ALU.mult, ALU.mult)
                return eng.scalar_tensor_tensor(bcn[0:16, L + c0:L + c0 + SW],
                                                cbuf[:, c0:c0 + SW],
                                                clnw[:, 0:1], pp[2][0:16, :],
                                                ALU.mult, ALU.mult)
            P.op("vector", normv, waits=[(s_t, t_rep)], inc=s_v)
        v_norm = s_v.v

        P.dma("gpsimd", bnH, bcn[0:16, 0:L], s_st, waits=[(s_v, v_norm)])
        P.dma("gpsimd", cnH, bcn[0:16, L:2 * L], s_st, waits=[(s_v, v_norm)])
        st_bc = s_st.v

        st_d_done = 0
        for dt in range(DT):
            P.dma("sync", wdtb[:, 0:128],
                  ap(i_wdtT, dt * 128, [[H, 128], [1, 128]]), s_ld,
                  waits=[(s_t, s_t.v)])
            P.dma("sync", wdtb[:, 128:256],
                  ap(i_wdtT, 128 * H + dt * 128, [[H, 128], [1, 128]]), s_ld)
            ld_w = s_ld.v
            for sl in range(4):
                c0 = sl * SW
                P.dma("sync", usl[:, :], ap(i_u, (dt * 128) * L + c0, [[L, 128], [1, SW]]),
                      s_ld, waits=[(s_v, s_v.v)])
                ld_u = s_ld.v

                def dmm(eng, c0=c0):
                    eng.matmul(pp[0][:, :], wdtb[:, 0:128], dtn0[:, c0:c0 + SW],
                               start=True, stop=False)
                    return eng.matmul(pp[0][:, :], wdtb[:, 128:256], dtn1[:, c0:c0 + SW],
                                      start=False, stop=True)
                P.op("tensor", dmm, waits=[(s_ld, ld_w), (s_v, s_v.v)], inc=s_t)
                t_dmm = s_t.v

                def vbias(eng, dt=dt):
                    return eng.scalar_tensor_tensor(
                        dsl[:, :], pp[0][:, :], 1.0,
                        dtbias[:, dt:dt + 1].broadcast_to([128, SW]),
                        ALU.mult, ALU.add)
                P.op("vector", vbias, waits=[(s_t, t_dmm), (s_st, st_d_done)], inc=s_v)
                v_bias = s_v.v

                def sp_s(eng):
                    eng.activation(t_a[:, :], dsl[:, :], AF.Abs)
                    eng.activation(t_b[:, :], t_a[:, :], AF.Exp, scale=-1.0)
                    return eng.activation(t_a[:, :], t_b[:, :], AF.Ln, bias=1.0)
                P.op("scalar", sp_s, waits=[(s_v, v_bias)], inc=s_s)
                s_spc = s_s.v

                def sp_v(eng):
                    eng.tensor_scalar_max(t_b[:, :], dsl[:, :], 0.0)
                    eng.tensor_tensor(dsl[:, :], t_b[:, :], t_a[:, :], ALU.add)
                    return eng.tensor_tensor(dusl[:, :], dsl[:, :], usl[:, :], ALU.mult)
                P.op("vector", sp_v, waits=[(s_s, s_spc), (s_ld, ld_u)], inc=s_v)
                v_d = s_v.v

                P.dma("gpsimd", ap(dH, (dt * 128) * L + c0, [[L, 128], [1, SW]]),
                      dsl[:, :], s_st, waits=[(s_v, v_d)])
                P.dma("gpsimd", ap(duH, (dt * 128) * L + c0, [[L, 128], [1, SW]]),
                      dusl[:, :], s_st, waits=[(s_v, v_d)])
                st_d_done = s_st.v

        P.op("vector", lambda eng: eng.memset(carry[:, :], 0.0),
             waits=[(s_st, st_d_done)], inc=s_v)
        v_carry0 = s_v.v

        st_z = {}
        for c in range(NCH):
            c0 = c * CH
            P.dma("sync", brep[:, :], ap(bnH, c0, [[0, 128], [L, N], [1, CH]]),
                  s_ld, waits=[(s_st, st_bc), (s_v, s_v.v)])
            P.dma("sync", crep[:, :], ap(cnH, c0, [[0, 128], [L, N], [1, CH]]),
                  s_ld, waits=[])
            ld_rep = s_ld.v
            for dt in range(DT):
                P.dma("sync", dchunk[:, :], ap(dH, dt * 128 * L + c0, [[L, 128], [1, CH]]),
                      s_ld, waits=[(s_v, s_v.v)])
                P.dma("sync", duchunk[:, :], ap(duH, dt * 128 * L + c0, [[L, 128], [1, CH]]),
                      s_ld)
                P.dma("sync", uchunk[:, :], ap(i_u, dt * 128 * L + c0, [[L, 128], [1, CH]]),
                      s_ld)
                P.dma("sync", sgchunk[:, :], ap(i_sg, dt * 128 * L + c0, [[L, 128], [1, CH]]),
                      s_ld)
                ld_ch = s_ld.v

                def das(eng, dt=dt):
                    last = None
                    for n in range(N):
                        last = eng.activation(dab[:, n * CH:(n + 1) * CH], dchunk[:, :],
                                              AF.Exp, scale=acol[:, dt * N + n:dt * N + n + 1])
                    return last
                P.op("scalar", das, waits=[(s_ld, ld_ch), (s_v, s_v.v)], inc=s_s)
                s_da = s_s.v

                def scanv(eng, dt=dt):
                    du_b = duchunk[:, :].unsqueeze(1).broadcast_to([128, N, CH])
                    eng.tensor_tensor(wb[:, :], du_b, brep[:, :], ALU.mult)
                    for n in range(N):
                        eng.tensor_tensor_scan(
                            stb[:, n * CH:(n + 1) * CH],
                            dab[:, n * CH:(n + 1) * CH],
                            wb[:, n * CH:(n + 1) * CH],
                            carry[:, dt * N + n:dt * N + n + 1],
                            ALU.mult, ALU.add)
                    eng.tensor_tensor(wb[:, :], stb[:, :], crep[:, :], ALU.mult)
                    tv = ap(wb[:, :], 0, [[N * CH, 128], [1, CH], [CH, N]])
                    eng.tensor_reduce(ychunk[:, :], tv, AX.X, ALU.add)
                    eng.tensor_copy(carry[:, dt * N:(dt + 1) * N],
                                    ap(stb[:, :], CH - 1, [[N * CH, 128], [CH, N]]))
                    eng.scalar_tensor_tensor(zchunk[:, :], uchunk[:, :],
                                             dpar[:, dt:dt + 1], ychunk[:, :],
                                             ALU.mult, ALU.add)
                    return eng.tensor_tensor(zchunk[:, :], zchunk[:, :], sgchunk[:, :],
                                             ALU.mult)
                waits = [(s_s, s_da), (s_ld, max(ld_ch, ld_rep)), (s_v, v_carry0)]
                key = (c, dt - 1) if dt > 0 else ((c - 1, DT - 1) if c > 0 else None)
                if key is not None:
                    waits.append((s_st, st_z[key]))
                P.op("vector", scanv, waits=waits, inc=s_v)
                v_scan = s_v.v

                P.dma("gpsimd", ap(o_z, dt * 128 * L + c0, [[L, 128], [1, CH]]),
                      zchunk[:, :], s_st, waits=[(s_v, v_scan)])
                st_z[(c, dt)] = s_st.v

        P.op("gpsimd", lambda eng: None, waits=[(s_st, s_st.v)])

        block = es.enter_context(nc.Block())
        P.build_block(block)

    return nc


def prep_p2(inputs, cid, l1_outs):
    b, dh = cid // 2, cid % 2
    d0 = dh * H
    partner = b * 2 + (1 - dh)
    own = l1_outs[cid]
    par = l1_outs[partner]
    Wdt = np.asarray(inputs["L0_attn_Wdt"], np.float32)
    A = np.asarray(inputs["L0_attn_A"], np.float32)
    m = {
        "u": np.ascontiguousarray(np.asarray(own["u"], np.float32)),
        "sg": np.ascontiguousarray(np.asarray(own["sg"], np.float32)),
        "spa0": np.asarray(own["sp0"], np.float32),
        "spa1": np.asarray(own["sp1"], np.float32),
        "spa2": np.asarray(own["sp2"], np.float32),
        "spb0": np.asarray(par["sp0"], np.float32),
        "spb1": np.asarray(par["sp1"], np.float32),
        "spb2": np.asarray(par["sp2"], np.float32),
        "dtlnw": np.ascontiguousarray(
            np.asarray(inputs["L0_attn_dt_ln_w"], np.float32).reshape(2, 128).T),
        "blnw": np.asarray(inputs["L0_attn_b_ln_w"], np.float32).reshape(16, 1),
        "clnw": np.asarray(inputs["L0_attn_c_ln_w"], np.float32).reshape(16, 1),
        "wdtT": np.ascontiguousarray(
            Wdt[d0:d0 + H, :].T.reshape(2, 128, H)),
        "dtbias": np.ascontiguousarray(
            np.asarray(inputs["L0_attn_dt_bias"], np.float32)[d0:d0 + H].reshape(DT, 128).T),
        "Acol": np.ascontiguousarray(
            A[d0:d0 + H].reshape(DT, 128, N).transpose(1, 0, 2).reshape(128, DT * N)),
        "dparam": np.ascontiguousarray(
            np.asarray(inputs["L0_attn_D_param"], np.float32)[d0:d0 + H].reshape(DT, 128).T),
        "onescol": np.ones((128, 1), np.float32),
        "onesrow": np.ones((1, 128), np.float32),
        "cvec": np.tile(np.array([[1e-5, 1e-6, 1.0, 0.0]], np.float32), (128, 1)),
    }
    return m



def build_p3(debug=False, skip_k=False):
    nc = bass.Bass("TRN2", target_bir_lowering=False)
    i_z = dram_in(nc, "zfull", [2 * H, T1])
    i_woT = dram_in(nc, "woT3", [ZT, 128, H])
    i_xn = dram_in(nc, "xnsl", [H, T1])
    i_x = dram_in(nc, "xsl", [H, T1])
    i_fw = dram_in(nc, "ffnT3", [HT, 128, H])
    i_fsc = dram_in(nc, "fscale", [1, 1024])
    i_rw = dram_in(nc, "routerT", [H, 4])
    i_lA = dram_in(nc, "lorAT", [H, 64])
    i_lB = dram_in(nc, "lorBT", [64, H])
    i_epat = dram_in(nc, "epat", [4, 64])
    i_lamrow = dram_in(nc, "lamrow", [1, 128])
    i_ct = dram_in(nc, "ctrow", [1, 128])
    i_pq = dram_in(nc, "pqrow", [1, 256])
    i_vcT = dram_in(nc, "vcT", [64, 128])
    i_bp = dram_in(nc, "bp", [64, 1])
    i_ls = dram_in(nc, "logstep", [1, 1])
    i_dsc = dram_in(nc, "dscal", [1, 1])
    i_wk = dram_in(nc, "wkc", [128, 64])
    i_ck = dram_in(nc, "ckc", [128, 64])
    i_e64 = dram_in(nc, "e64", [64, 192])
    i_tw = dram_in(nc, "tw", [64, 128])
    i_jf = dram_in(nc, "jflip", [128, 128])
    i_id = dram_in(nc, "ident", [128, 128])
    i_oc = dram_in(nc, "onescol", [128, 1])
    i_or = dram_in(nc, "onesrow", [1, 128])
    i_cv = dram_in(nc, "cvec", [128, 4])

    o_out = dram_out(nc, "out", [H, T1])

    kind = dict(kind="ExternalOutput") if debug else {}
    gH = nc.dram_tensor("gH", [H, T1], F32, **kind).ap()
    sH = nc.dram_tensor("sH", [H, T1], F32, **kind).ap()
    x1H = nc.dram_tensor("x1H", [H, T1], F32, **kind).ap()
    g1H = nc.dram_tensor("g1H", [H, T1], F32, **kind).ap()
    arHr = nc.dram_tensor("arHr", [4096], F32, **kind).ap()
    arHi = nc.dram_tensor("arHi", [4096], F32, **kind).ap()
    kH = nc.dram_tensor("kH", [4223], F32, **kind).ap()
    dbg1 = nc.dram_tensor("dbg_arena", [128, 16384], F32, **kind).ap() if debug else None
    dbg2 = nc.dram_tensor("dbg_screp", [128, 1024], F32, **kind).ap() if debug else None
    dbg3 = nc.dram_tensor("dbg_wcol", [128, 4096], F32, **kind).ap() if debug else None

    with contextlib.ExitStack() as es:
        sb = lambda name, shp, dt=F32: es.enter_context(nc.sbuf_tensor(name, shp, dt))
        ps = lambda name, shp: es.enter_context(nc.psum_tensor(name, shp, F32))

        arena = sb("arena", [128, 32768])
        wcol = sb("wcol", [128, 6144])
        ttl = sb("ttl", [128, 32 * 128])
        rows = sb("rows3", [1, 4096])
        fsc_v = rows[0:1, 3072:4096]
        ocol = sb("ocol3", [128, 1])
        orow = sb("orow3", [1, 128])
        cv = sb("cv3", [128, 4])
        idm = sb("idm", [128, 128])
        jf = sb("jf", [128, 128])
        stile = sb("stile", [128, T1])
        stile2 = sb("stile2", [128, T1])
        screp = stile2
        cden = arena[:, 0:4096]
        crec = arena[:, 4096:8192]
        ctmp = arena[:, 8192:10240]
        _ko = [10240]
        def kv(width, rows=128):
            o = _ko[0]; _ko[0] += width
            return arena[0:rows, o:o + width]
        wk = kv(64)
        ckc = kv(64)
        e64 = kv(192, 64)
        tw = kv(128, 64)
        vcT = kv(128, 64)
        bp = kv(1, 64)
        lrow = kv(128, 1)
        ctrow = kv(128, 1)
        pqrow = kv(256, 1)
        vrows = kv(512, 1)
        srow = kv(16, 1)
        lamrep = kv(128)
        vrep = kv(512)
        wks = kv(64)
        kbuf = kv(256)
        abuf = kv(256)
        ksb = kv(64, 64)
        zx = kv(128, 64)
        zxT = kv(128, 64)
        hl = kv(128)
        zrow = kv(127, 1)
        assert _ko[0] <= 16384
        s2col = sb("s2col", [128, 1])
        dcol = sb("dcol", [128, 1])
        pk = sb("pk", [128, T1])
        stile3 = pk
        lgt = sb("lgt", [4, T1])
        gtT = sb("gtT", [4, T1])

        mids = pk[0:64, :]
        lgT = sb("lgT", [128, 32])
        gsc = sb("gsc", [128, 64])
        gm1 = sb("gm1", [128, 8])
        gm1n = sb("gm1n", [128, 8])
        gm2 = sb("gm2", [128, 8])
        gmask = sb("gmask", [128, 32])
        gexpb = sb("gexpb", [128, 32])
        rwsb = sb("rwsb", [128, HT * 4])
        lAsb = wcol[:, 4096:6144]
        lBsb = wcol[0:64, 0:4096]
        epat = sb("epat_s", [4, 64])
        colb = sb("colb", [128, 16])

        pp = [ps(f"pp{i}", [128, 512]) for i in range(8)]

        P = Prog(nc, es)
        s_ld = P.sem("s_ld")
        s_st = P.sem("s_st")
        s_t = P.sem("s_t")
        s_v = P.sem("s_v")
        s_s = P.sem("s_s")

        def BAR():
            return P.barrier_waits()

        for dst, src in [(ocol, i_oc), (orow, i_or), (cv, i_cv), (idm, i_id),
                         (jf, i_jf), (wk, i_wk), (ckc, i_ck),
                         (e64, i_e64), (tw, i_tw), (vcT, i_vcT), (bp, i_bp),
                         (lrow, i_lamrow), (ctrow, i_ct), (pqrow, i_pq),
                         (epat, i_epat)]:
            P.dma("sync", dst[:, :], src, s_ld)
        P.dma("sync", srow[:, 0:1], i_ls, s_ld)
        P.dma("sync", srow[:, 1:2], i_dsc, s_ld)
        P.dma("sync", rwsb[:, :], ap(i_rw, 0, [[4, 128], [128 * 4, HT], [1, 4]]), s_ld)
        ld_pre = s_ld.v

        if not skip_k:
            P.op("scalar", lambda eng: eng.activation(srow[:, 2:3], srow[:, 0:1], AF.Exp),
                 waits=[(s_ld, ld_pre)], inc=s_s)
            P.op("vector", lambda eng: eng.reciprocal(srow[:, 3:4], srow[:, 2:3]),
                 waits=[(s_s, s_s.v)], inc=s_v)
            P.op("scalar", lambda eng: eng.mul(srow[:, 4:5], srow[:, 3:4], 2.0),
                 waits=[(s_v, s_v.v)], inc=s_s)
            P.op("vector", lambda eng: eng.engine_nop(), waits=[(s_s, s_s.v)], inc=s_v)
            def bc_cols(eng):
                eng.matmul(pp[0][:, 0:1], orow[:, :], srow[:, 4:5], start=True, stop=True)
                return eng.matmul(pp[1][:, 0:1], orow[:, :], srow[:, 1:2], start=True, stop=True)
            P.op("tensor", bc_cols, waits=[(s_v, s_v.v)], inc=s_t)
            P.op("vector", lambda eng: (eng.tensor_copy(s2col[:, :], pp[0][:, 0:1]),
                                        eng.tensor_copy(dcol[:, :], pp[1][:, 0:1]))[-1],
                 waits=[(s_t, s_t.v)], inc=s_v)

            def lam_mm(eng):
                return eng.matmul(pp[2][:, 0:128], orow[:, :], lrow[0:1, :],
                                  start=True, stop=True)
            P.op("tensor", lam_mm, waits=[(s_v, s_v.v)], inc=s_t)
            P.op("vector", lambda eng: eng.tensor_copy(lamrep[:, :], pp[2][:, 0:128]),
                 waits=[(s_t, s_t.v)], inc=s_v)

            def bmm(eng):
                eng.matmul(pp[0][0:64, 0:1], vcT[:, 0:64], bp[:, :], start=True, stop=True)
                return eng.matmul(pp[1][0:64, 0:1], vcT[:, 64:128], bp[:, :],
                                  start=True, stop=True)
            P.op("tensor", bmm, waits=[(s_v, s_v.v)], inc=s_t)
            P.op("vector", lambda eng: (eng.tensor_copy(colb[0:64, 0:1], pp[0][0:64, 0:1]),
                                        eng.tensor_copy(colb[0:64, 1:2], pp[1][0:64, 0:1]))[-1],
                 waits=[(s_t, s_t.v)], inc=s_v)

            def btr(eng):
                eng.transpose(pp[3][0:1, 0:64], colb[0:64, 0:1], idm[0:64, 0:64])
                return eng.transpose(pp[4][0:1, 0:64], colb[0:64, 1:2], idm[0:64, 0:64])
            P.op("tensor", btr, waits=[(s_v, s_v.v)], inc=s_t)
            P.op("vector", lambda eng: (eng.tensor_copy(rows[0:1, 0:64], pp[3][0:1, 0:64]),
                                        eng.tensor_copy(rows[0:1, 64:128], pp[4][0:1, 0:64]))[-1],
                 waits=[(s_t, s_t.v)], inc=s_v)
            v_brow = s_v.v

            def vv(eng):
                ctr = ctrow[0:1, 0:64]
                cti = ctrow[0:1, 64:128]
                pr = pqrow[0:1, 0:64]
                pi = pqrow[0:1, 64:128]
                qr = pqrow[0:1, 128:192]
                qi = pqrow[0:1, 192:256]
                br = rows[0:1, 0:64]
                bi = rows[0:1, 64:128]
                t0 = rows[0:1, 512:576]
                t1 = rows[0:1, 576:640]
                for j, (ar_, ai_, br_, bi_) in enumerate(
                        [(ctr, cti, br, bi), (ctr, cti, pr, pi),
                         (qr, qi, br, bi), (qr, qi, pr, pi)]):
                    re_o = vrows[0:1, j * 128:j * 128 + 64]
                    im_o = vrows[0:1, j * 128 + 64:j * 128 + 128]
                    eng.tensor_tensor(t0, ar_, br_, ALU.mult)
                    eng.tensor_tensor(t1, ai_, bi_, ALU.mult)
                    eng.tensor_tensor(re_o, t0, t1, ALU.add)
                    eng.tensor_tensor(t0, ar_, bi_, ALU.mult)
                    eng.tensor_tensor(t1, ai_, br_, ALU.mult)
                    eng.tensor_tensor(im_o, t0, t1, ALU.subtract)
                return eng.engine_nop()
            P.op("vector", vv, waits=[(s_v, v_brow)], inc=s_v)
            v_vv = s_v.v

            def vrep_seq():
                for j in range(8):
                    def mm1(eng, j=j):
                        return eng.matmul(pp[5][:, 0:64], orow[:, :],
                                          vrows[0:1, j * 64:(j + 1) * 64],
                                          start=True, stop=True)
                    P.op("tensor", mm1, waits=[(s_v, s_v.v)], inc=s_t)

                    def cp1(eng, j=j):
                        return eng.tensor_copy(vrep[:, j * 64:(j + 1) * 64], pp[5][:, 0:64])
                    P.op("vector", cp1, waits=[(s_t, s_t.v)], inc=s_v)
            vrep_seq()
            v_vrep = s_v.v

            def denb(eng):
                eng.tensor_scalar_mul(wks[:, :], wk[:, :], s2col[:, 0:1])
                gre = vap(wks[:, :], 0, [[2, 32], [0, 64]])
                gim = vap(wks[:, :], 1, [[2, 32], [0, 64]])
                lre = vap(lamrep[:, :], 0, [[0, 32], [1, 64]])
                lim = vap(lamrep[:, :], 64, [[0, 32], [1, 64]])
                eng.tensor_tensor(cden[:, 0:2048], gre, lre, ALU.subtract)
                eng.tensor_tensor(cden[:, 2048:4096], gim, lim, ALU.subtract)
                eng.tensor_tensor(ctmp[:, :], cden[:, 0:2048], cden[:, 0:2048], ALU.mult)
                eng.scalar_tensor_tensor(crec[:, 0:2048], cden[:, 2048:4096], 1.0,
                                         cden[:, 2048:4096], ALU.mult, ALU.mult)
                eng.tensor_tensor(ctmp[:, :], ctmp[:, :], crec[:, 0:2048], ALU.add)
                eng.reciprocal(crec[:, 2048:4096], ctmp[:, :])
                eng.tensor_tensor(crec[:, 0:2048], cden[:, 0:2048], crec[:, 2048:4096],
                                  ALU.mult)
                eng.tensor_tensor(ctmp[:, :], cden[:, 2048:4096], crec[:, 2048:4096],
                                  ALU.mult)
                return eng.tensor_scalar_mul(crec[:, 2048:4096], ctmp[:, :], -1.0)
            P.op("vector", denb, waits=[(s_v, v_vrep)], inc=s_v)
            v_den = s_v.v

            def cauchy(eng):
                for j in range(4):
                    vr = vap(vrep[:, :], j * 128, [[0, 32], [1, 64]])
                    vi = vap(vrep[:, :], j * 128 + 64, [[0, 32], [1, 64]])
                    rr = crec[:, 0:2048]
                    ri = crec[:, 2048:4096]
                    eng.tensor_tensor(ctmp[:, :], vr, rr, ALU.mult)
                    eng.tensor_tensor(cden[:, 0:2048], vi, ri, ALU.mult)
                    eng.tensor_tensor(ctmp[:, :], ctmp[:, :], cden[:, 0:2048], ALU.subtract)
                    tv = vap(ctmp[:, :], 0, [[64, 32], [1, 64]])
                    eng.tensor_reduce(kbuf[:, j * 64:j * 64 + 32], tv, AX.X, ALU.add)
                    eng.tensor_tensor(ctmp[:, :], vr, ri, ALU.mult)
                    eng.tensor_tensor(cden[:, 0:2048], vi, rr, ALU.mult)
                    eng.tensor_tensor(ctmp[:, :], ctmp[:, :], cden[:, 0:2048], ALU.add)
                    tv = vap(ctmp[:, :], 0, [[64, 32], [1, 64]])
                    eng.tensor_reduce(kbuf[:, j * 64 + 32:j * 64 + 64], tv, AX.X, ALU.add)
                return eng.engine_nop()
            P.op("vector", cauchy, waits=[(s_v, v_den)], inc=s_v)
            v_cau = s_v.v

            def atroots(eng):
                k00r = kbuf[:, 0:32]; k00i = kbuf[:, 32:64]
                k01r = kbuf[:, 64:96]; k01i = kbuf[:, 96:128]
                k10r = kbuf[:, 128:160]; k10i = kbuf[:, 160:192]
                k11r = kbuf[:, 192:224]; k11i = kbuf[:, 224:256]
                wr = abuf[:, 0:32]; wi = abuf[:, 32:64]
                d2 = abuf[:, 64:96]; rc = abuf[:, 96:128]
                nr = abuf[:, 128:160]; ni = abuf[:, 160:192]
                t0 = abuf[:, 192:224]; t1 = abuf[:, 224:256]
                eng.tensor_scalar_add(wr, k11r, 1.0)
                eng.tensor_copy(wi, k11i)
                eng.tensor_tensor(t0, k01r, k10r, ALU.mult)
                eng.tensor_tensor(t1, k01i, k10i, ALU.mult)
                eng.tensor_tensor(nr, t0, t1, ALU.subtract)
                eng.tensor_tensor(t0, k01r, k10i, ALU.mult)
                eng.tensor_tensor(t1, k01i, k10r, ALU.mult)
                eng.tensor_tensor(ni, t0, t1, ALU.add)
                eng.tensor_tensor(d2, wr, wr, ALU.mult)
                eng.tensor_tensor(t0, wi, wi, ALU.mult)
                eng.tensor_tensor(d2, d2, t0, ALU.add)
                eng.tensor_tensor(t0, nr, wr, ALU.mult)
                eng.tensor_tensor(t1, ni, wi, ALU.mult)
                eng.reciprocal(rc, d2)
                eng.tensor_tensor(t0, t0, t1, ALU.add)
                eng.tensor_tensor(t0, t0, rc, ALU.mult)
                eng.tensor_tensor(t1, ni, wr, ALU.mult)
                eng.tensor_tensor(d2, nr, wi, ALU.mult)
                eng.tensor_tensor(t1, t1, d2, ALU.subtract)
                eng.tensor_tensor(t1, t1, rc, ALU.mult)
                eng.tensor_tensor(t0, k00r, t0, ALU.subtract)
                eng.tensor_tensor(t1, k00i, t1, ALU.subtract)
                ckr = vap(ckc[:, :], 0, [[2, 32]])
                cki = vap(ckc[:, :], 1, [[2, 32]])
                eng.tensor_tensor(nr, ckr, t0, ALU.mult)
                eng.tensor_tensor(ni, cki, t1, ALU.mult)
                eng.tensor_tensor(abuf[:, 0:32], nr, ni, ALU.subtract)
                eng.tensor_tensor(nr, ckr, t1, ALU.mult)
                eng.tensor_tensor(ni, cki, t0, ALU.mult)
                return eng.tensor_tensor(abuf[:, 32:64], nr, ni, ALU.add)
            P.op("vector", atroots, waits=[(s_v, v_cau)], inc=s_v)
            v_ar = s_v.v

            P.dma("gpsimd", ap(arHr, 0, [[1, 128], [128, 32]]), abuf[:, 0:32], s_st,
                  waits=[(s_v, v_ar)], slow=True)
            P.dma("gpsimd", ap(arHi, 0, [[1, 128], [128, 32]]), abuf[:, 32:64], s_st,
                  waits=[(s_v, v_ar)], slow=True)
            st_ar = s_st.v
            P.dma("sync", hl[0:64, 0:64], ap(arHr, 0, [[64, 64], [1, 64]]), s_ld,
                  waits=[(s_st, st_ar)])
            P.dma("sync", hl[0:64, 64:128], ap(arHi, 0, [[64, 64], [1, 64]]), s_ld)
            ld_X = s_ld.v

            def st1(eng):
                eng.matmul(pp[0][0:64, 0:64], e64[:, 0:64], hl[0:64, 0:64],
                           start=True, stop=False)
                eng.matmul(pp[0][0:64, 0:64], e64[:, 64:128], hl[0:64, 64:128],
                           start=False, stop=True)
                eng.matmul(pp[1][0:64, 0:64], e64[:, 0:64], hl[0:64, 64:128],
                           start=True, stop=False)
                return eng.matmul(pp[1][0:64, 0:64], e64[:, 128:192], hl[0:64, 0:64],
                                  start=False, stop=True)
            P.op("tensor", st1, waits=[(s_ld, ld_X), (s_v, s_v.v)], inc=s_t)
            t_st1 = s_t.v

            def twid(eng):
                eng.tensor_tensor(zx[:, 0:64], pp[0][0:64, 0:64], tw[:, 0:64], ALU.mult)
                eng.tensor_tensor(ksb[:, :], pp[1][0:64, 0:64], tw[:, 64:128], ALU.mult)
                eng.tensor_tensor(zx[:, 0:64], zx[:, 0:64], ksb[:, :], ALU.subtract)
                eng.tensor_tensor(zx[:, 64:128], pp[0][0:64, 0:64], tw[:, 64:128], ALU.mult)
                eng.tensor_tensor(ksb[:, :], pp[1][0:64, 0:64], tw[:, 0:64], ALU.mult)
                return eng.tensor_tensor(zx[:, 64:128], zx[:, 64:128], ksb[:, :], ALU.add)
            P.op("vector", twid, waits=[(s_t, t_st1)], inc=s_v)
            v_tw = s_v.v

            def ztr(eng):
                eng.transpose(pp[2][0:64, 0:64], zx[0:64, 0:64], idm[0:64, 0:64])
                return eng.transpose(pp[3][0:64, 0:64], zx[0:64, 64:128], idm[0:64, 0:64])
            P.op("tensor", ztr, waits=[(s_v, v_tw)], inc=s_t)
            P.op("vector", lambda eng: (eng.tensor_copy(zxT[:, 0:64], pp[2][0:64, 0:64]),
                                        eng.tensor_copy(zxT[:, 64:128], pp[3][0:64, 0:64]))[-1],
                 waits=[(s_t, s_t.v)], inc=s_v)
            v_ztr = s_v.v

            def st2(eng):
                eng.matmul(pp[4][0:64, 0:64], e64[:, 0:64], zxT[0:64, 0:64],
                           start=True, stop=False)
                return eng.matmul(pp[4][0:64, 0:64], e64[:, 64:128], zxT[0:64, 64:128],
                                  start=False, stop=True)
            P.op("tensor", st2, waits=[(s_v, v_ztr)], inc=s_t)
            P.op("vector", lambda eng: (eng.memset(zrow[:, :], 0.0),
                                        eng.tensor_scalar_mul(ksb[:, :], pp[4][0:64, 0:64],
                                                              1.0 / 4096.0))[-1],
                 waits=[(s_t, s_t.v)], inc=s_v)
            v_k = s_v.v
            P.dma("gpsimd", ap(kH, 0, [[1, 127]]), zrow[0:1, :], s_st, waits=[(s_v, v_k)])
            P.dma("gpsimd", ap(kH, 127, [[64, 64], [1, 64]]), ksb[0:64, :], s_st,
                  waits=[(s_v, v_k)])
            st_k = s_st.v

            for d in range(32):
                P.dma("sync", hl[:, :], ap(kH, 128 * d, [[1, 128], [1, 128]]), s_ld,
                      waits=[(s_st, st_k), (s_t, s_t.v)])
                ld_h = s_ld.v

                def flip(eng):
                    return eng.matmul(pp[5][:, 0:128], jf[:, :], hl[:, :],
                                      start=True, stop=True)
                P.op("tensor", flip, waits=[(s_ld, ld_h), (s_v, s_v.v)], inc=s_t)

                def ev(eng, d=d):
                    return eng.tensor_copy(ttl[:, d * 128:(d + 1) * 128], pp[5][:, 0:128])
                P.op("vector", ev, waits=[(s_t, s_t.v)], inc=s_v)
            v_ttl = s_v.v

        else:
            P.op("vector", lambda eng: eng.memset(ttl[:, :], 0.0), inc=s_v)
            P.op("vector", lambda eng: eng.memset(dcol[:, :], 0.0), inc=s_v)
            v_ttl = s_v.v

        P.dma("sync", fsc_v, i_fsc, s_ld, waits=[(s_v, s_v.v)])
        ld_fsc = s_ld.v
        def screp_mm(eng):
            eng.matmul(pp[0][:, :], orow[:, :], fsc_v[0:1, 0:512], start=True, stop=True)
            return eng.matmul(pp[1][:, :], orow[:, :], fsc_v[0:1, 512:1024],
                              start=True, stop=True)
        P.op("tensor", screp_mm, waits=[(s_ld, ld_fsc), (s_v, s_v.v)], inc=s_t)
        P.op("vector", lambda eng: (eng.tensor_copy(screp[:, 0:512], pp[0][:, :]),
                                    eng.tensor_copy(screp[:, 512:1024], pp[1][:, :]))[-1],
             waits=[(s_t, s_t.v)], inc=s_v)
        v_screp = s_v.v

        st_g_done = 0
        for sl in range(2):
            c0 = sl * SW
            for h in range(HT):
                w_ = []
                if h == 0:
                    w_ = [(s_v, v_ttl)] if sl == 0 else [(s_t, s_t.v)]
                P.dma("sync", arena[:, h * SW:(h + 1) * SW],
                      ap(i_xn, (h * 128) * T1 + c0, [[T1, 128], [1, SW]]), s_ld,
                      waits=w_)
            ld_xn = s_ld.v
            for oi in range(HT):
                P.dma("sync", wcol[:, 0:HT * 128],
                      ap(i_fw, oi * 128, [[H, 128], [128 * H, HT], [1, 128]]), s_ld,
                      waits=[(s_v, s_v.v), (s_t, s_t.v)])
                ld_w = s_ld.v

                def wsc(eng, oi=oi):
                    last = None
                    for ib in range(HT):
                        last = eng.tensor_scalar_mul(
                            wcol[:, ib * 128:(ib + 1) * 128],
                            wcol[:, ib * 128:(ib + 1) * 128],
                            screp[:, ib * 32 + oi:ib * 32 + oi + 1])
                    return last
                P.op("vector", wsc, waits=[(s_ld, ld_w), (s_v, v_screp)], inc=s_v)
                v_wsc = s_v.v

                if debug and sl == 0 and oi == 0:
                    P.dma("gpsimd", dbg1, arena[:, 0:16384], s_st,
                          waits=[(s_ld, ld_xn), (s_v, s_v.v)])
                    P.dma("gpsimd", dbg2, screp[:, :], s_st, waits=[(s_v, s_v.v)])
                    P.dma("gpsimd", dbg3, wcol[:, 0:4096], s_st, waits=[(s_v, s_v.v)])

                def gmm(eng):
                    last = None
                    for ib in range(HT):
                        last = eng.matmul(pp[2][:, :], wcol[:, ib * 128:(ib + 1) * 128],
                                          arena[:, ib * SW:(ib + 1) * SW],
                                          start=(ib == 0), stop=(ib == HT - 1))
                    return last
                P.op("tensor", gmm, waits=[(s_v, v_wsc), (s_ld, ld_xn),
                                           (s_s, s_s.v)], inc=s_t)
                t_gmm = s_t.v

                def gsilu(eng, c0=c0):
                    return eng.activation(stile[:, c0:c0 + SW], pp[2][:, :], AF.Silu)
                P.op("scalar", gsilu, waits=[(s_t, t_gmm), (s_st, st_g_done)], inc=s_s)
                s_gs = s_s.v
                P.dma("gpsimd", ap(gH, (oi * 128) * T1 + c0, [[T1, 128], [1, SW]]),
                      stile[:, c0:c0 + SW], s_st, waits=[(s_s, s_gs)])
                st_g_done = s_st.v

        st_s_done = 0
        for sl in range(2):
            c0 = sl * SW
            for h in range(ZT):
                P.dma("sync", arena[:, h * SW:(h + 1) * SW],
                      ap(i_z, (h * 128) * T1 + c0, [[T1, 128], [1, SW]]), s_ld,
                      waits=[(s_t, s_t.v)] if h == 0 else [])
            ld_z = s_ld.v
            for oi in range(HT):
                t_smm = None
                for half in range(2):
                    P.dma("sync", wcol[:, 0:4096],
                          ap(i_woT, half * 32 * 128 * H + oi * 128,
                             [[H, 128], [128 * H, 32], [1, 128]]), s_ld,
                          waits=[(s_t, s_t.v)])
                    ld_w = s_ld.v

                    def smm(eng, half=half):
                        last = None
                        for ib in range(32):
                            last = eng.matmul(
                                pp[3][:, :], wcol[:, ib * 128:(ib + 1) * 128],
                                arena[:, (half * 32 + ib) * SW:(half * 32 + ib + 1) * SW],
                                start=(half == 0 and ib == 0),
                                stop=(half == 1 and ib == 31))
                        return last
                    P.op("tensor", smm, waits=[(s_ld, max(ld_w, ld_z)), (s_s, s_s.v)],
                         inc=s_t)
                    t_smm = s_t.v

                def sev(eng, c0=c0):
                    return eng.copy(stile2[:, c0:c0 + SW], pp[3][:, :])
                P.op("scalar", sev, waits=[(s_t, t_smm), (s_st, st_s_done)], inc=s_s)
                s_se = s_s.v
                P.dma("gpsimd", ap(sH, (oi * 128) * T1 + c0, [[T1, 128], [1, SW]]),
                      stile2[:, c0:c0 + SW], s_st, waits=[(s_s, s_se)])
                st_s_done = s_st.v

        ld_mark = {}
        for h in range(HT):
            P.dma("sync", stile[:, 0:T1], ap(sH, (h * 128) * T1, [[T1, 128], [1, T1]]),
                  s_ld, waits=([(s_st, st_s_done)] if h == 0 else []) +
                             [(s_v, s_v.v), (s_t, s_t.v)])
            P.dma("sync", stile2[:, 0:T1], ap(gH, (h * 128) * T1, [[T1, 128], [1, T1]]),
                  s_ld)
            P.dma("sync", stile3[:, 0:T1], ap(i_x, (h * 128) * T1, [[T1, 128], [1, T1]]),
                  s_ld)
            ld_h = s_ld.v

            def x1v(eng, h=h):
                eng.tensor_tensor(stile[:, 0:T1], stile[:, 0:T1], stile2[:, 0:T1],
                                  ALU.mult)
                return eng.tensor_tensor(arena[:, h * T1:(h + 1) * T1], stile[:, 0:T1],
                                         stile3[:, 0:T1], ALU.add)
            P.op("vector", x1v, waits=[(s_ld, ld_h), (s_s, s_s.v)], inc=s_v)
            v_x1 = s_v.v

            def x1sq(eng, h=h):
                return eng.activation(stile2[:, 0:T1], arena[:, h * T1:(h + 1) * T1],
                                      AF.Square)
            P.op("scalar", x1sq, waits=[(s_v, v_x1), (s_t, s_t.v)], inc=s_s)
            s_x1q = s_s.v

            def x1mm(eng, h=h):
                st = (h == 0)
                sp = (h == HT - 1)
                eng.matmul(pp[4][0:1, :], ocol[:, :], arena[:, h * T1:h * T1 + 512],
                           start=st, stop=sp)
                eng.matmul(pp[5][0:1, :], ocol[:, :], arena[:, h * T1 + 512:(h + 1) * T1],
                           start=st, stop=sp)
                eng.matmul(pp[6][0:1, :], ocol[:, :], stile2[:, 0:512],
                           start=st, stop=sp)
                return eng.matmul(pp[7][0:1, :], ocol[:, :], stile2[:, 512:1024],
                                  start=st, stop=sp)
            P.op("tensor", x1mm, waits=[(s_s, s_x1q)], inc=s_t)
        t_stats1 = s_t.v

        r_m = rows[0:1, 0:T1]
        r_sq = rows[0:1, T1:2 * T1]
        r_rstd = rows[0:1, 2 * T1:3 * T1]

        def stat1v(eng):
            eng.tensor_scalar_mul(r_m[0:1, 0:512], pp[4][0:1, :], 1.0 / H)
            eng.tensor_scalar_mul(r_m[0:1, 512:1024], pp[5][0:1, :], 1.0 / H)
            eng.tensor_scalar_mul(r_sq[0:1, 0:512], pp[6][0:1, :], 1.0 / H)
            eng.tensor_scalar_mul(r_sq[0:1, 512:1024], pp[7][0:1, :], 1.0 / H)
            eng.tensor_tensor(rows[0:1, 3 * T1:4 * T1], r_m, r_m, ALU.mult)
            return eng.tensor_tensor(r_sq, r_sq, rows[0:1, 3 * T1:4 * T1], ALU.subtract)
        P.op("vector", stat1v, waits=[(s_t, t_stats1)], inc=s_v)
        v_st1 = s_v.v

        def rstd1(eng):
            eng.activation(rows[0:1, 3 * T1:4 * T1], r_sq, AF.Ln, bias=cv[0:1, 0:1])
            return eng.activation(r_rstd, rows[0:1, 3 * T1:4 * T1], AF.Exp, scale=-0.5)
        P.op("scalar", rstd1, waits=[(s_v, v_st1)], inc=s_s)
        s_r1 = s_s.v

        def rep1(eng):
            eng.matmul(pp[4][:, :], orow[:, :], r_m[0:1, 0:512], start=True, stop=True)
            eng.matmul(pp[5][:, :], orow[:, :], r_m[0:1, 512:1024], start=True, stop=True)
            eng.matmul(pp[6][:, :], orow[:, :], r_rstd[0:1, 0:512], start=True, stop=True)
            return eng.matmul(pp[7][:, :], orow[:, :], r_rstd[0:1, 512:1024],
                              start=True, stop=True)
        P.op("tensor", rep1, waits=[(s_s, s_r1)], inc=s_t)
        t_rep1 = s_t.v

        st_x1 = 0
        for h in range(HT):
            P.dma("gpsimd", ap(x1H, (h * 128) * T1, [[T1, 128], [1, T1]]),
                  arena[:, h * T1:(h + 1) * T1], s_st, waits=[(s_t, t_rep1)])
            st_x1 = s_st.v

            def xn1v(eng, h=h):
                a = arena[:, h * T1:(h + 1) * T1]
                eng.tensor_tensor(stile[:, 0:512], a[:, 0:512], pp[4][:, :], ALU.subtract)
                eng.tensor_tensor(stile[:, 512:1024], a[:, 512:1024], pp[5][:, :],
                                  ALU.subtract)
                eng.tensor_tensor(a[:, 0:512], stile[:, 0:512], pp[6][:, :], ALU.mult)
                return eng.tensor_tensor(a[:, 512:1024], stile[:, 512:1024], pp[7][:, :],
                                         ALU.mult)
            P.op("vector", xn1v, waits=[(s_st, st_x1), (s_s, s_s.v)], inc=s_v)
        v_xn1 = s_v.v

        P.dma("sync", lAsb, ap(i_lA, 0, [[64, 128], [128 * 64, HT], [1, 64]]),
              s_ld, waits=[(s_t, s_t.v)])
        P.dma("sync", lBsb, i_lB, s_ld)
        ld_lor = s_ld.v
        P.op("tensor", lambda eng: None, waits=[(s_ld, ld_lor)])
        for sl2 in range(2):
            def rmm1(eng, sl2=sl2):
                last = None
                for h in range(HT):
                    last = eng.matmul(pp[0][0:4, :], rwsb[:, h * 4:(h + 1) * 4],
                                      arena[:, h * T1 + sl2 * 512:h * T1 + (sl2 + 1) * 512],
                                      start=(h == 0), stop=(h == HT - 1))
                return last
            P.op("tensor", rmm1, waits=[(s_v, v_xn1 if sl2 == 0 else s_v.v)], inc=s_t)
            t_r = s_t.v

            def rev(eng, sl2=sl2):
                return eng.tensor_copy(lgt[:, sl2 * 512:(sl2 + 1) * 512], pp[0][0:4, :])
            P.op("vector", rev, waits=[(s_t, t_r)], inc=s_v)
        v_lg = s_v.v

        for blk in range(8):
            def ltr(eng, blk=blk):
                return eng.transpose(pp[1][0:128, blk * 4:(blk + 1) * 4],
                                     lgt[0:4, blk * 128:(blk + 1) * 128],
                                     idm[0:4, 0:4])
            P.op("tensor", ltr, waits=[(s_v, s_v.v)], inc=s_t)
        t_ltr = s_t.v
        P.op("vector", lambda eng: eng.tensor_copy(lgT[:, :], pp[1][0:128, 0:32]),
             waits=[(s_t, t_ltr)], inc=s_v)

        def pass_mins(eng):
            for blk in range(8):
                lt = lgT[:, blk * 4:(blk + 1) * 4]
                k = 0
                for i in range(4):
                    for j in range(i + 1, 4):
                        eng.tensor_tensor(gsc[:, blk * 8 + k:blk * 8 + k + 1],
                                          lt[:, i:i + 1], lt[:, j:j + 1], ALU.min)
                        k += 1
                eng.tensor_reduce(gm1[:, blk:blk + 1], lt, AX.X, ALU.max)
            return eng.engine_nop()
        P.op("vector", pass_mins, waits=[(s_v, s_v.v)], inc=s_v)

        def pass_m2(eng):
            for blk in range(8):
                eng.tensor_reduce(gm2[:, blk:blk + 1],
                                  gsc[:, blk * 8:blk * 8 + 6], AX.X, ALU.max)
            for blk in range(8):
                eng.tensor_scalar_mul(gm1n[:, blk:blk + 1], gm1[:, blk:blk + 1], -1.0)
            return eng.engine_nop()
        P.op("vector", pass_m2, waits=[(s_v, s_v.v)], inc=s_v)

        def pass_mask(eng):
            for blk in range(8):
                eng.tensor_tensor(gmask[:, blk * 4:(blk + 1) * 4],
                                  lgT[:, blk * 4:(blk + 1) * 4],
                                  gm2[:, blk:blk + 1].broadcast_to([128, 4]), ALU.is_ge)
            return eng.engine_nop()
        P.op("vector", pass_mask, waits=[(s_v, s_v.v)], inc=s_v)
        v_mask = s_v.v

        def pass_exp(eng):
            last = None
            for blk in range(8):
                last = eng.activation(gexpb[:, blk * 4:(blk + 1) * 4],
                                      lgT[:, blk * 4:(blk + 1) * 4], AF.Exp,
                                      bias=gm1n[:, blk:blk + 1])
            return last
        P.op("scalar", pass_exp, waits=[(s_v, v_mask)], inc=s_s)
        s_gexp = s_s.v

        def pass_em(eng):
            for blk in range(8):
                eng.tensor_tensor(gexpb[:, blk * 4:(blk + 1) * 4],
                                  gexpb[:, blk * 4:(blk + 1) * 4],
                                  gmask[:, blk * 4:(blk + 1) * 4], ALU.mult)
            for blk in range(8):
                eng.tensor_reduce(gm1[:, blk:blk + 1],
                                  gexpb[:, blk * 4:(blk + 1) * 4], AX.X, ALU.add)
            for blk in range(8):
                eng.reciprocal(gm2[:, blk:blk + 1], gm1[:, blk:blk + 1])
            for blk in range(8):
                eng.tensor_scalar_mul(lgT[:, blk * 4:(blk + 1) * 4],
                                      gexpb[:, blk * 4:(blk + 1) * 4],
                                      gm2[:, blk:blk + 1])
            return eng.engine_nop()
        P.op("vector", pass_em, waits=[(s_s, s_gexp)], inc=s_v)
        v_gts = s_v.v

        for blk in range(8):
            def gtr(eng, blk=blk):
                return eng.transpose(pp[1][0:4, 0:128], lgT[:, blk * 4:(blk + 1) * 4],
                                     idm[:, :])
            P.op("tensor", gtr, waits=[(s_v, v_gts if blk == 0 else s_v.v)], inc=s_t)
            t_gtr = s_t.v
            P.op("vector",
                 (lambda blk=blk: (lambda eng: eng.tensor_copy(
                     gtT[:, blk * 128:(blk + 1) * 128], pp[1][0:4, 0:128])))(),
                 waits=[(s_t, t_gtr)], inc=s_v)
        v_gates = s_v.v

        for sl2 in range(2):
            def gre_mm(eng, sl2=sl2):
                return eng.matmul(pp[2][0:64, :], epat[:, :],
                                  gtT[:, sl2 * 512:(sl2 + 1) * 512],
                                  start=True, stop=True)
            P.op("tensor", gre_mm, waits=[(s_v, v_gates), (s_v, s_v.v)], inc=s_t)
            t_gr = s_t.v

            def mid_mm(eng, sl2=sl2):
                last = None
                for h in range(HT):
                    last = eng.matmul(pp[3][0:64, :], lAsb[:, h * 64:(h + 1) * 64],
                                      arena[:, h * T1 + sl2 * 512:h * T1 + (sl2 + 1) * 512],
                                      start=(h == 0), stop=(h == HT - 1))
                return last
            P.op("tensor", mid_mm, waits=[(s_v, v_xn1)], inc=s_t)
            t_mid = s_t.v

            def midp(eng, sl2=sl2):
                eng.tensor_copy(mids[:, sl2 * 512:(sl2 + 1) * 512], pp[3][0:64, :])
                return eng.tensor_tensor(mids[:, sl2 * 512:(sl2 + 1) * 512],
                                         mids[:, sl2 * 512:(sl2 + 1) * 512],
                                         pp[2][0:64, :], ALU.mult)
            P.op("vector", midp, waits=[(s_t, t_mid), (s_t, t_gr)], inc=s_v)
        v_mid = s_v.v

        st_g1 = 0
        for oi in range(HT):
            for sl2 in range(2):
                def d1mm(eng, oi=oi, sl2=sl2):
                    return eng.matmul(pp[4][:, :], lBsb[:, oi * 128:(oi + 1) * 128],
                                      mids[:, sl2 * 512:(sl2 + 1) * 512],
                                      start=True, stop=True)
                P.op("tensor", d1mm, waits=[(s_v, v_mid), (s_s, s_s.v)], inc=s_t)
                t_d1 = s_t.v

                def g1s(eng, sl2=sl2):
                    return eng.activation(stile[:, sl2 * 512:(sl2 + 1) * 512],
                                          pp[4][:, :], AF.Silu)
                P.op("scalar", g1s, waits=[(s_t, t_d1), (s_st, st_g1)], inc=s_s)
                s_g1 = s_s.v
                P.dma("gpsimd", ap(g1H, (oi * 128) * T1 + sl2 * 512, [[T1, 128], [1, 512]]),
                      stile[:, sl2 * 512:(sl2 + 1) * 512], s_st, waits=[(s_s, s_g1)])
                st_g1 = s_st.v

        for oi in range(HT):
            ld_fin = {}
            P.dma("sync", stile2[:, 0:T1], ap(g1H, (oi * 128) * T1, [[T1, 128], [1, T1]]),
                  s_ld, waits=[(s_st, s_st.v), (s_v, s_v.v)])
            P.dma("sync", stile3[:, 0:T1], ap(x1H, (oi * 128) * T1, [[T1, 128], [1, T1]]),
                  s_ld)
            ld_f = s_ld.v
            for sl2 in range(2):
                def tmm(eng, oi=oi, sl2=sl2):
                    last = None
                    for ji in range(oi + 1):
                        last = eng.matmul(pp[6][:, :],
                                          ttl[:, (oi - ji) * 128:(oi - ji + 1) * 128],
                                          arena[:, ji * T1 + sl2 * 512:ji * T1 + (sl2 + 1) * 512],
                                          start=(ji == 0), stop=(ji == oi))
                    return last
                P.op("tensor", tmm, waits=[(s_v, s_v.v), (s_s, s_s.v)], inc=s_t)
                t_t = s_t.v

                def fin(eng, oi=oi, sl2=sl2):
                    c = sl2 * 512
                    eng.scalar_tensor_tensor(stile[:, c:c + 512],
                                             arena[:, oi * T1 + c:oi * T1 + c + 512],
                                             dcol[:, 0:1], pp[6][:, :],
                                             ALU.mult, ALU.add)
                    eng.tensor_tensor(stile[:, c:c + 512], stile[:, c:c + 512],
                                      stile2[:, c:c + 512], ALU.mult)
                    return eng.tensor_tensor(stile[:, c:c + 512], stile[:, c:c + 512],
                                             stile3[:, c:c + 512], ALU.add)
                P.op("vector", fin, waits=[(s_t, t_t), (s_ld, ld_f), (s_st, s_st.v)],
                     inc=s_v)
                v_f = s_v.v
                P.dma("gpsimd", ap(o_out, (oi * 128) * T1 + sl2 * 512,
                                   [[T1, 128], [1, 512]]),
                      stile[:, sl2 * 512:(sl2 + 1) * 512], s_st, waits=[(s_v, v_f)])

        P.op("gpsimd", lambda eng: None, waits=[(s_st, s_st.v)])

        block = es.enter_context(nc.Block())
        P.build_block(block)

    return nc


def _mkconsts():
    L = 4096
    kk = (np.arange(L) / np.float32(L)).astype(np.float32)
    Omega = np.exp(np.complex64(-2j * np.pi) * kk.astype(np.complex64))
    Wk = ((1.0 - Omega) / (1.0 + Omega)).astype(np.complex64)
    ck = (2.0 / (1.0 + Omega)).astype(np.complex64)
    n1 = np.arange(64)
    E64 = np.exp(2j * np.pi * n1[:, None] * n1[None, :] / 64).astype(np.complex64)
    twc = np.exp(2j * np.pi * n1[:, None] * n1[None, :] / L).astype(np.complex64)

    def c2cols(z):
        zr = z.real.astype(np.float32).reshape(32, 128).T
        zi = z.imag.astype(np.float32).reshape(32, 128).T
        out = np.empty((128, 64), np.float32)
        out[:, 0::2] = zr
        out[:, 1::2] = zi
        return out

    e64 = np.concatenate([E64.real, -E64.imag, E64.imag], axis=1).astype(np.float32)
    tw = np.concatenate([twc.real, twc.imag], axis=1).astype(np.float32)
    jf = np.zeros((128, 128), np.float32)
    jf[np.arange(128), 127 - np.arange(128)] = 1.0
    return c2cols(Wk), c2cols(ck), e64, tw, jf


def prep_p3(inputs, cid, l2_z, l1_xn):
    b, dh = cid // 2, cid % 2
    t0 = dh * T1
    c0 = b * 2
    zfull = np.concatenate([np.asarray(l2_z[c0], np.float32)[:, t0:t0 + T1],
                            np.asarray(l2_z[c0 + 1], np.float32)[:, t0:t0 + T1]], axis=0)
    Wo = np.asarray(inputs["L0_attn_W_out"], np.float32)
    Fw = np.asarray(inputs["L0_ffn_weight"], np.float32)
    Fs = np.asarray(inputs["L0_ffn_weight_scale"], np.float32)
    Rw = np.asarray(inputs["L1_ffn_router_w"], np.float32)
    lA = np.asarray(inputs["L1_ffn_A"], np.float32)
    lB = np.asarray(inputs["L1_ffn_B"], np.float32)
    x = np.asarray(inputs["L0_attn_hidden_states"], np.float32)
    wkc, ckc, e64, tw, jf = _mkconsts()
    lam = np.asarray(inputs["L1_attn_Lambda_ri"], np.float32)
    pri = np.asarray(inputs["L1_attn_p_ri"], np.float32)
    qri = np.asarray(inputs["L1_attn_q_ri"], np.float32)
    vc = np.asarray(inputs["L1_attn_Vc_ri"], np.float32)
    ct = np.asarray(inputs["L1_attn_Ct_ri"], np.float32).reshape(64, 2)
    epat = np.zeros((4, 64), np.float32)
    for e in range(4):
        epat[e, e * 16:(e + 1) * 16] = 4.0
    m = {
        "zfull": np.ascontiguousarray(zfull),
        "woT3": np.ascontiguousarray(Wo.T.reshape(ZT, 128, H)),
        "xnsl": np.ascontiguousarray(np.asarray(l1_xn[cid], np.float32)[:, t0:t0 + T1]),
        "xsl": np.ascontiguousarray(x[b].T[:, t0:t0 + T1]),
        "ffnT3": np.ascontiguousarray(Fw.T.reshape(HT, 128, H)),
        "fscale": np.ascontiguousarray(Fs.T.reshape(1, 1024)),
        "routerT": np.ascontiguousarray(Rw.T),
        "lorAT": np.ascontiguousarray(lA.reshape(64, H).T),
        "lorBT": np.ascontiguousarray(lB.transpose(0, 2, 1).reshape(64, H)),
        "epat": epat,
        "lamrow": np.ascontiguousarray(
            np.concatenate([lam[:, 0], lam[:, 1]]).reshape(1, 128)),
        "ctrow": np.ascontiguousarray(
            np.concatenate([ct[:, 0], ct[:, 1]]).reshape(1, 128)),
        "pqrow": np.ascontiguousarray(
            np.concatenate([pri[:, 0], pri[:, 1], qri[:, 0], qri[:, 1]]).reshape(1, 256)),
        "vcT": np.ascontiguousarray(
            np.concatenate([vc[:, :, 0].T, vc[:, :, 1].T], axis=1)),
        "bp": np.asarray(inputs["L1_attn_Bp"], np.float32),
        "logstep": np.asarray(inputs["L1_attn_log_step"], np.float32).reshape(1, 1),
        "dscal": np.asarray(inputs["L1_attn_D"], np.float32).reshape(1, 1),
        "wkc": wkc, "ckc": ckc, "e64": e64, "tw": tw, "jflip": jf,
        "ident": np.eye(128, dtype=np.float32),
        "onescol": np.ones((128, 1), np.float32),
        "onesrow": np.ones((1, 128), np.float32),
        "cvec": np.tile(np.array([[1e-5, 1e-6, 1.0, 0.0]], np.float32), (128, 1)),
    }
    return m

def kernel(**inputs):
    n = 8
    nc1 = build_p1()
    maps1 = [prep_p1(inputs, c) for c in range(n)]
    r1 = run_bass_kernel_spmd(nc1, maps1, core_ids=list(range(n)))
    l1 = [r1.results[c] for c in range(n)]

    nc2 = build_p2()
    maps2 = [prep_p2(inputs, c, l1) for c in range(n)]
    r2 = run_bass_kernel_spmd(nc2, maps2, core_ids=list(range(n)))
    l2z = [r2.results[c]["z"] for c in range(n)]
    l1xn = [l1[c]["xn"] for c in range(n)]

    nc3 = build_p3()
    maps3 = [prep_p3(inputs, c, l2z, l1xn) for c in range(n)]
    r3 = run_bass_kernel_spmd(nc3, maps3, core_ids=list(range(n)))

    out = np.empty((4, 2048, 4096), np.float32)
    for cid in range(n):
        b, dh = cid // 2, cid % 2
        t0 = dh * T1
        out[b, t0:t0 + T1, :] = np.asarray(r3.results[cid]["out"], np.float32).T
    return out
